# version 01
# baseline (speedup 1.0000x reference)
import contextlib
import math

import numpy as np

import concourse.bass as bass
import concourse.mybir as mybir
from concourse.bass_utils import run_bass_kernel_spmd

F32 = mybir.dt.float32
BF16 = mybir.dt.bfloat16
I16 = mybir.dt.int16
U16 = mybir.dt.uint16
AF = mybir.ActivationFunctionType
ALU = mybir.AluOpType
AX = mybir.AxisListType

B, S, H, NH, HD, II = 4, 4096, 2048, 16, 128, 5632
NTOK = 2048
NTT = NTOK // 128
HC = H // 128
IC = II // 128
ISL = II // 512
KT = S // 128

TWO_PI = 2.0 * math.pi
C2PI_HI = float(np.float32(TWO_PI))
C2PI_LO = float(np.float32(np.float64(TWO_PI) - np.float64(np.float32(TWO_PI))))
PI_HI = float(np.float32(math.pi))
PI_LO = float(np.float32(np.float64(math.pi) - np.float64(np.float32(math.pi))))
HPI_HI = float(np.float32(math.pi / 2))
HPI_LO = float(np.float32(np.float64(math.pi / 2) - np.float64(np.float32(math.pi / 2))))
INV_2PI = float(np.float32(1.0 / TWO_PI))
LN1E4 = math.log(10000.0)
LN_HI = float(np.float32(LN1E4))
LN_LO = float(np.float32(np.float64(LN1E4) - np.float64(np.float32(LN1E4))))

ENGS = ("te", "ve", "se", "ge", "sy")


class Sem:
    def __init__(self, nc, es, name):
        self.h = es.enter_context(nc.semaphore(name))
        self.n = 0


class Prog:

    def __init__(self, nc, es):
        self.nc = nc
        self.ops = {e: [] for e in ENGS}
        self.sems = {}
        for nm in ("LD", "SP", "G2", "PE", "VE", "SE", "BAR"):
            self.sems[nm] = Sem(nc, es, "s" + nm)

    def app(self, eng, fn):
        self.ops[eng].append(fn)

    def wait(self, eng, sem, n=None):
        v = sem.n if n is None else n
        self.app(eng, lambda e: e.wait_ge(sem.h, v))
        return v

    def op(self, eng, fn, inc=None, k=1):
        if inc is not None:
            inc.n += k
            def g(e, fn=fn, inc=inc, k=k):
                fn(e).then_inc(inc.h, k)
        else:
            def g(e, fn=fn):
                fn(e)
        self.app(eng, g)
        if inc is not None:
            return inc.n

    def vsop(self, fn):
        n = self.op("ve", fn, inc=self.sems["VE"])
        self.wait("ve", self.sems["VE"], n)
        return n

    def dma(self, eng, out, in_, sem):
        return self.op(eng, lambda e: e.dma_start(out=out, in_=in_), inc=sem, k=16)

    def barrier(self):
        bar = self.sems["BAR"]
        bar.n += len(ENGS)
        tgt = bar.n
        for eng in ENGS:
            def f(e, tgt=tgt):
                e.nop().then_inc(bar.h, 1)
                e.wait_ge(bar.h, tgt)
            self.app(eng, f)

    def emit(self, block):
        def run(e, lst):
            for f in lst:
                f(e)
        block.tensor(lambda e: run(e, self.ops["te"]))
        block.vector(lambda e: run(e, self.ops["ve"]))
        block.scalar(lambda e: run(e, self.ops["se"]))
        block.gpsimd(lambda e: run(e, self.ops["ge"]))
        block.sync(lambda e: run(e, self.ops["sy"]))


def emit_qfp8(P, r_ap, q_ap, tmp, ncols, inc=None):
    npart = r_ap.shape[0]
    sl = (slice(0, npart), slice(0, ncols))
    P.op("ve", lambda v: v.tensor_scalar(
        out=tmp["sb"][sl], in0=r_ap.bitcast(U16), scalar1=0x7F80, scalar2=None,
        op0=ALU.bitwise_and))
    P.op("ve", lambda v: v.tensor_scalar(
        out=tmp["sb"][sl], in0=tmp["sb"][sl], scalar1=0x3C80, scalar2=0x180,
        op0=ALU.max, op1=ALU.subtract))
    P.op("ve", lambda v: v.tensor_scalar(
        out=tmp["inv"][sl], in0=tmp["sb"][sl], scalar1=-1, scalar2=0x7F00,
        op0=ALU.mult, op1=ALU.add))
    P.op("ve", lambda v: v.tensor_tensor(
        out=tmp["u"][sl], in0=r_ap, in1=tmp["inv"][sl].bitcast(BF16), op=ALU.mult))
    P.op("ve", lambda v: v.tensor_copy(out=tmp["n16"][sl], in_=tmp["u"][sl]))
    P.op("ve", lambda v: v.tensor_copy(out=tmp["nf"][sl], in_=tmp["n16"][sl]))
    P.op("ve", lambda v: v.scalar_tensor_tensor(
        out=tmp["c1"][sl], in0=tmp["nf"][sl], scalar=0.5, in1=tmp["u"][sl],
        op0=ALU.add, op1=ALU.is_lt))
    P.op("ve", lambda v: v.scalar_tensor_tensor(
        out=tmp["c2"][sl], in0=tmp["nf"][sl], scalar=-0.5, in1=tmp["u"][sl],
        op0=ALU.add, op1=ALU.is_gt))
    P.op("ve", lambda v: v.tensor_tensor(
        out=tmp["nf"][sl], in0=tmp["nf"][sl], in1=tmp["c1"][sl], op=ALU.add))
    P.op("ve", lambda v: v.tensor_tensor(
        out=tmp["nf"][sl], in0=tmp["nf"][sl], in1=tmp["c2"][sl], op=ALU.subtract))
    P.op("ve", lambda v: v.tensor_tensor(
        out=q_ap, in0=tmp["nf"][sl], in1=tmp["sb"][sl].bitcast(BF16),
        op=ALU.mult), inc=inc)


def qfp8_tmps(T, name, ncols):
    return dict(sb=T(name + "_sb", [128, ncols], U16),
                inv=T(name + "_inv", [128, ncols], U16),
                u=T(name + "_u", [128, ncols], BF16),
                n16=T(name + "_n16", [128, ncols], I16),
                nf=T(name + "_nf", [128, ncols], BF16),
                c1=T(name + "_c1", [128, ncols], BF16),
                c2=T(name + "_c2", [128, ncols], BF16))


def emit_ln(P, xsrc, dst, bufs, csc_unused=None):
    m, v, m2, rstd, sq = bufs["m"], bufs["v"], bufs["m2"], bufs["rstd"], bufs["sq"]
    P.vsop(lambda e: e.reduce_sum(out=m[:], in_=xsrc, axis=AX.X))
    vsq = P.vsop(lambda e: e.tensor_scalar(out=m[:], in0=m[:], scalar1=1.0 / 2048,
                                           scalar2=None, op0=ALU.mult))
    P.wait("se", P.sems["VE"], vsq)
    ssq = P.op("se", lambda e: e.activation(out=sq[:], in_=xsrc, func=AF.Square),
               inc=P.sems["SE"])
    P.wait("ve", P.sems["SE"], ssq)
    P.vsop(lambda e: e.reduce_sum(out=v[:], in_=sq[:], axis=AX.X))
    P.vsop(lambda e: e.tensor_scalar(out=v[:], in0=v[:], scalar1=1.0 / 2048,
                                     scalar2=None, op0=ALU.mult))
    P.vsop(lambda e: e.tensor_tensor(out=m2[:], in0=m[:], in1=m[:], op=ALU.mult))
    P.vsop(lambda e: e.tensor_tensor(out=v[:], in0=v[:], in1=m2[:], op=ALU.subtract))
    vv = P.vsop(lambda e: e.tensor_scalar(out=v[:], in0=v[:], scalar1=1e-5,
                                          scalar2=None, op0=ALU.add))
    P.wait("se", P.sems["VE"], vv)
    sv = P.op("se", lambda e: e.activation(out=v[:], in_=v[:], func=AF.Sqrt),
              inc=P.sems["SE"])
    P.wait("ve", P.sems["SE"], sv)
    P.vsop(lambda e: e.reciprocal(out=rstd[:], in_=v[:]))
    return P.op("ve", lambda e: e.tensor_scalar(
        out=dst, in0=xsrc, scalar1=m[:], scalar2=rstd[:],
        op0=ALU.subtract, op1=ALU.mult), inc=P.sems["VE"])


def build_program(last_phase="out", taps=(), qkv_parts="qkv", dbg=()):
    PHASES = ["tables", "xfm", "qkv", "wq", "attn", "oproj", "ffna", "ffnb", "out"]
    phase_n = PHASES.index(last_phase)

    def want(ph):
        return PHASES.index(ph) <= phase_n

    nc = bass.Bass("TRN2", target_bir_lowering=False, detect_race_conditions=False)

    def dram(name, shape, dt, kind="ExternalInput"):
        return nc.dram_tensor(name, shape, dt, kind=kind).ap()

    xT = dram("xT", [H, S], F32)
    xqT = dram("xqT", [H, NTOK], F32)
    xrows = dram("xrows", [NTOK, H], F32)
    wqT = dram("wqT", [H, H], F32)
    wkT = dram("wkT", [H, H], F32)
    wvT = dram("wvT", [H, H], F32)
    woT = dram("woT", [H, H], F32)
    wgT = dram("wgT", [H, II], F32)
    wuT = dram("wuT", [H, II], F32)
    wdT = dram("wdT", [II, H], F32)
    ident_in = dram("ident", [128, 128], F32)
    ones_in = dram("ones", [128, 128], F32)
    iota_t = dram("iota_t", [64, S], F32)
    iota_q = dram("iota_q", [64, NTOK], F32)
    iota_j = dram("iota_j", [64, 1], F32)
    bmask = dram("bmask", [4, 128, 512], F32)

    y_out = dram("y", [NTOK, H], F32, kind="ExternalOutput")

    def scr(name, shape, dt=BF16):
        kind = "ExternalOutput" if name in taps else "Internal"
        return dram(name, shape, dt, kind=kind)

    xfm_s = scr("xfm_s", [HC, 128, S])
    xqfm_s = scr("xqfm_s", [HC, 128, NTOK])
    q_s = scr("q_s", [NH, 128, NTOK])
    k_s = scr("k_s", [NH, 128, S])
    v_s = scr("v_s", [NH, 128, S])
    ctx_s = scr("ctx_s", [NH, 128, NTOK]) if "ctx_s" in taps else None
    x1_s = scr("x1_s", [NTT, 128, H])
    qx_s = scr("qx_s", [HC, 128, NTOK])
    t_s = scr("t_s", [NTT, 128, II])
    qwg_s = scr("qwg_s", [HC, 128, II])
    qwu_s = scr("qwu_s", [HC, 128, II])
    qwd_s = scr("qwd_s", [IC, 128, H])
    wsg_s = scr("wsg_s", [1, II], F32)
    wsu_s = scr("wsu_s", [1, II], F32)
    wsd_s = scr("wsd_s", [1, H], F32)
    rws_d = scr("rws_d", [1, II])
    rope_s = scr("rope_s", [4, 128, S]) if "rope_s" in taps else None

    es = contextlib.ExitStack()
    with es:
        T = lambda nm, sh, dt: es.enter_context(nc.sbuf_tensor(nm, sh, dt))
        ident_bf = T("ident_bf", [128, 128], BF16)
        ident_f = T("ident_f", [128, 128], F32)
        ones_bf = T("ones_bf", [128, 128], BF16)
        ones_f = T("ones_f", [128, 128], F32)
        csc = T("csc", [128, 1], F32)
        ax_col = T("ax_col", [128, NTT], F32)
        at_col = T("at_col", [128, NTT], F32)
        rmx_t = T("rmx_t", [128, NTT], F32)

        pb = [es.enter_context(nc.psum_tensor(f"pb{i}", [128, 512], F32))
              for i in range(8)]

        P = Prog(nc, es)
        sLD, sSP, sG2 = P.sems["LD"], P.sems["SP"], P.sems["G2"]
        sPE, sVE, sSE = P.sems["PE"], P.sems["VE"], P.sems["SE"]

        rope_stack = contextlib.ExitStack()
        Tr = lambda nm, sh, dt: rope_stack.enter_context(nc.sbuf_tensor(nm, sh, dt))
        ropeC = Tr("ropeC", [128, S], BF16)
        ropeS = Tr("ropeS", [128, S], BF16)
        ropeCq = Tr("ropeCq", [128, NTOK], BF16)
        ropeSq = Tr("ropeSq", [128, NTOK], BF16)

        with contextlib.ExitStack() as ps:
            Tp = lambda nm, sh, dt: ps.enter_context(nc.sbuf_tensor(nm, sh, dt))
            cst1 = Tp("cst1", [128, 128], F32)
            cst2 = Tp("cst2", [128, 128], F32)
            fj = Tp("fj", [64, 1], F32)
            Lc = Tp("Lc", [64, 1], F32)
            th_sb = Tp("th_sb", [64, 512], F32)
            th2 = Tp("th2", [64, 512], F32)
            rrd = Tp("rrd", [64, 512], F32)
            rrn = Tp("rrn", [64, 512], F32)
            rrc = Tp("rrc", [64, 512], F32)
            rrphi = Tp("rrphi", [64, 512], F32)
            rr16 = Tp("rr16", [64, 512], I16)
            sinv = Tp("sinv", [64, 512], F32)
            ios = Tp("ios", [64, 512], F32)

            P.dma("sy", cst1[:], ident_in, sLD)
            P.dma("sy", cst2[:], ones_in, sLD)
            P.dma("sy", fj[:], iota_j, sLD)
            P.wait("ve", sLD)
            P.op("ve", lambda e: e.tensor_copy(out=ident_bf[:], in_=cst1[:]))
            P.op("ve", lambda e: e.tensor_copy(out=ident_f[:], in_=cst1[:]))
            P.op("ve", lambda e: e.tensor_copy(out=ones_bf[:], in_=cst2[:]))
            P.op("ve", lambda e: e.tensor_copy(out=ones_f[:], in_=cst2[:]),
                 inc=sVE)
            vcs = P.op("ve", lambda e: e.memset(csc[:], 0.0078125), inc=sVE)
            P.wait("se", sLD)
            P.wait("se", sVE, vcs)
            P.op("se", lambda e: e.activation(out=csc[:], in_=csc[:],
                                              func=AF.Sqrt), inc=sSE)
            P.wait("ve", sLD)
            P.vsop(lambda e: e.tensor_scalar(
                out=fj[:], in0=fj[:], scalar1=-1.0 / 64, scalar2=None,
                op0=ALU.mult))
            P.vsop(lambda e: e.tensor_scalar(
                out=Lc[:], in0=fj[:], scalar1=LN_HI, scalar2=None, op0=ALU.mult))
            vL = P.vsop(lambda e: e.scalar_tensor_tensor(
                out=Lc[:], in0=fj[:], scalar=LN_LO, in1=Lc[:],
                op0=ALU.mult, op1=ALU.add))
            P.wait("se", sVE, vL)
            sF = P.op("se", lambda e: e.activation(out=fj[:], in_=Lc[:],
                                                   func=AF.Exp), inc=sSE)
            P.wait("ve", sSE, sF)
            vF = P.op("ve", lambda e: e.nop(), inc=sVE)

            def rope_chunk(src_iota, ch, Cdst, Sdst):
                P.wait("sy", sVE)
                P.dma("sy", ios[:], src_iota[:, ch * 512:(ch + 1) * 512], sLD)
                P.wait("ve", sLD)
                P.wait("ve", sSE, sF)
                P.op("ve", lambda e: e.tensor_scalar(
                    out=th_sb[:], in0=ios[:], scalar1=fj[:], scalar2=None,
                    op0=ALU.mult), inc=sVE)
                for trig in (0, 1):
                    if trig == 0:
                        P.op("ve", lambda e: e.tensor_copy(out=th2[:], in_=th_sb[:]))
                    else:
                        P.op("ve", lambda e: e.tensor_scalar(
                            out=th2[:], in0=th_sb[:], scalar1=HPI_HI,
                            scalar2=HPI_LO, op0=ALU.add, op1=ALU.add))
                    P.op("ve", lambda e: e.tensor_scalar(
                        out=rrd[:], in0=th2[:], scalar1=INV_2PI, scalar2=None,
                        op0=ALU.mult))
                    P.op("ve", lambda e: e.tensor_copy(out=rr16[:], in_=rrd[:]))
                    P.op("ve", lambda e: e.tensor_copy(out=rrn[:], in_=rr16[:]))
                    P.op("ve", lambda e: e.tensor_tensor(
                        out=rrc[:], in0=rrn[:], in1=rrd[:], op=ALU.is_gt))
                    P.op("ve", lambda e: e.tensor_tensor(
                        out=rrn[:], in0=rrn[:], in1=rrc[:], op=ALU.subtract))
                    P.op("ve", lambda e: e.scalar_tensor_tensor(
                        out=rrphi[:], in0=rrn[:], scalar=-C2PI_HI, in1=th2[:],
                        op0=ALU.mult, op1=ALU.add))
                    P.op("ve", lambda e: e.scalar_tensor_tensor(
                        out=rrphi[:], in0=rrn[:], scalar=-C2PI_LO, in1=rrphi[:],
                        op0=ALU.mult, op1=ALU.add))
                    P.op("ve", lambda e: e.tensor_scalar(
                        out=rrphi[:], in0=rrphi[:], scalar1=-PI_HI, scalar2=-PI_LO,
                        op0=ALU.add, op1=ALU.add))
                    vphi = P.op("ve", lambda e: e.tensor_scalar(
                        out=rrphi[:], in0=rrphi[:], scalar1=-PI_HI, scalar2=PI_HI,
                        op0=ALU.max, op1=ALU.min), inc=sVE)
                    P.wait("se", sVE, vphi)
                    ssin = P.op("se", lambda e: e.activation(
                        out=sinv[:], in_=rrphi[:], func=AF.Sin), inc=sSE)
                    P.wait("ve", sSE, ssin)
                    dst = Sdst if trig == 0 else Cdst
                    cols = slice(ch * 512, (ch + 1) * 512)
                    P.op("ve", lambda e, dst=dst, cols=cols: e.tensor_scalar(
                        out=dst[0:64, cols], in0=sinv[:], scalar1=-1.0,
                        scalar2=None, op0=ALU.mult))
                    P.op("ve", lambda e, dst=dst, cols=cols: e.tensor_copy(
                        out=dst[64:128, cols], in_=dst[0:64, cols]), inc=sVE)

            for ch in range(S // 512):
                rope_chunk(iota_t, ch, ropeC, ropeS)
            for ch in range(NTOK // 512):
                rope_chunk(iota_q, ch, ropeCq, ropeSq)

            if rope_s is not None:
                P.wait("ge", sVE)
                P.dma("ge", rope_s[0], ropeC[:], sSP)
                P.dma("ge", rope_s[1], ropeS[:], sSP)
                P.dma("ge", rope_s[2][:, 0:NTOK], ropeCq[:], sSP)
                P.dma("ge", rope_s[3][:, 0:NTOK], ropeSq[:], sSP)
            P.barrier()

        if want("xfm"):
            with contextlib.ExitStack() as ps:
                Tp = lambda nm, sh, dt: ps.enter_context(nc.sbuf_tensor(nm, sh, dt))
                xst = [Tp(f"xst{i}", [128, S], F32) for i in range(2)]
                xbf = [Tp(f"xbf{i}", [128, S], BF16) for i in range(2)]
                casts = []
                spills = []
                for j in range(2 * HC):
                    hc, qpart = j % HC, j // HC
                    src = xT[hc * 128:(hc + 1) * 128, :] if qpart == 0 else \
                        xqT[hc * 128:(hc + 1) * 128, :]
                    ncol = S if qpart == 0 else NTOK
                    dstd = xfm_s[hc] if qpart == 0 else xqfm_s[hc]
                    b = j % 2
                    if j >= 2:
                        P.wait("sy", sVE, casts[j - 2])
                    P.dma("sy", xst[b][:, 0:ncol], src, sLD)
                    P.wait("ve", sLD)
                    if j >= 2:
                        P.wait("ve", sSP, spills[j - 2])
                    casts.append(P.op("ve", lambda e, b=b, ncol=ncol: e.tensor_copy(
                        out=xbf[b][:, 0:ncol], in_=xst[b][:, 0:ncol]), inc=sVE))
                    P.wait("ge", sVE, casts[-1])
                    P.dma("ge", dstd, xbf[b][:, 0:ncol], sSP)
                    spills.append(sSP.n)
                P.wait("ge", sSP)
                P.barrier()

        if want("qkv"):
            with contextlib.ExitStack() as ps:
                Tp = lambda nm, sh, dt: ps.enter_context(nc.sbuf_tensor(nm, sh, dt))
                W_sb = [Tp(f"Wsb{i}", [128, H], BF16) for i in range(HC)]
                wst = [Tp(f"wst{i}", [128, H], F32) for i in range(2)]
                xqb = [[Tp(f"xqb{b}_{i}", [128, 1024], BF16) for i in range(HC)]
                       for b in range(2)]
                qb = [Tp(f"qbb{i}", [128, 512], BF16) for i in range(2)]
                tc_ = Tp("tc", [128, 512], BF16)
                ts_ = Tp("ts", [128, 512], BF16)
                tsw = Tp("tsw", [128, 512], BF16)
                ro = [Tp(f"ro{i}", [128, 512], BF16) for i in range(4)]
                vb = [Tp(f"vbb{i}", [128, 512], BF16) for i in range(4)]

                ro_dma = [0] * 4
                vb_dma = [0] * 4
                ro_i = [0]
                vb_i = [0]
                qb_se = [0, 0]
                qb_ve = [0, 0]

                def load_W(wsrc):
                    casts = []
                    pe_before = sPE.n
                    for hc in range(HC):
                        b = hc % 2
                        if hc >= 2:
                            P.wait("sy", sVE, casts[hc - 2])
                        if hc < 2:
                            P.wait("sy", sPE, pe_before)
                        P.dma("sy", wst[b][:], wsrc[hc * 128:(hc + 1) * 128, :], sLD)
                        P.wait("ve", sLD)
                        casts.append(P.op(
                            "ve", lambda e, hc=hc, b=b: e.tensor_copy(
                                out=W_sb[hc][:], in_=wst[b][:]), inc=sVE))
                    return sVE.n

                def rope_evac(bank, h, gcol0, own_tabs, dst_dram, dcols, mmdone):
                    i2 = h % 2
                    P.wait("se", sPE, mmdone)
                    P.wait("se", sVE, qb_ve[i2])
                    qb_se[i2] = P.op("se", lambda e, bank=bank, i2=i2: e.activation(
                        out=qb[i2][:], in_=bank[:], func=AF.Copy), inc=sSE)
                    Ct = (ropeCq if own_tabs else ropeC)
                    St = (ropeSq if own_tabs else ropeS)
                    tcols = slice(gcol0, gcol0 + 512)
                    P.wait("ve", sSE, qb_se[i2])
                    slot = ro_i[0]
                    P.wait("ve", sSP, ro_dma[slot])
                    if "norope" in dbg:
                        vdone = P.op("ve", lambda e, i2=i2, slot=slot: e.tensor_copy(
                            out=ro[slot][:], in_=qb[i2][:]), inc=sVE)
                        qb_ve[i2] = sVE.n
                        P.wait("ge", sVE, vdone)
                        P.dma("ge", dst_dram[:, dcols], ro[slot][:], sSP)
                        ro_dma[slot] = sSP.n
                        ro_i[0] = (slot + 1) % 4
                        return
                    P.op("ve", lambda e, i2=i2, tcols=tcols, Ct=Ct: e.tensor_tensor(
                        out=tc_[:], in0=qb[i2][:], in1=Ct[:, tcols], op=ALU.mult))
                    P.op("ve", lambda e, i2=i2, tcols=tcols, St=St: e.tensor_tensor(
                        out=ts_[:], in0=qb[i2][:], in1=St[:, tcols], op=ALU.mult))
                    qb_ve[i2] = sVE.n + 1
                    P.op("ve", lambda e: e.tensor_copy(
                        out=tsw[0:64, :], in_=ts_[64:128, :]))
                    P.op("ve", lambda e: e.tensor_copy(
                        out=tsw[64:128, :], in_=ts_[0:64, :]), inc=sVE)
                    P.op("ve", lambda e, slot=slot: e.tensor_tensor(
                        out=ro[slot][0:64, :], in0=tc_[0:64, :], in1=tsw[0:64, :],
                        op=ALU.subtract))
                    vdone = P.op("ve", lambda e, slot=slot: e.tensor_tensor(
                        out=ro[slot][64:128, :], in0=tc_[64:128, :],
                        in1=tsw[64:128, :], op=ALU.add), inc=sVE)
                    P.wait("ge", sVE, vdone)
                    P.dma("ge", dst_dram[:, dcols], ro[slot][:], sSP)
                    ro_dma[slot] = sSP.n
                    ro_i[0] = (slot + 1) % 4

                for wsrc, kind in ((wqT, "q"), (wkT, "k"), (wvT, "v")):
                    if kind not in qkv_parts:
                        continue
                    wdone = load_W(wsrc)
                    nquart = 2 if kind == "q" else 4
                    for qt in range(nquart):
                        b = qt % 2
                        pe_mark = sPE.n
                        xsrc_s = xqfm_s if kind == "q" else xfm_s
                        for hc in range(HC):
                            if qt >= 2:
                                P.wait("sy", sPE, pe_mark)
                            P.dma("sy", xqb[b][hc][:],
                                  xsrc_s[hc][:, qt * 1024:(qt + 1) * 1024], sG2)
                        xld = sG2.n
                        if kind in ("q", "k"):
                            for h in range(NH):
                                for ch in range(2):
                                    bank = pb[(h * 2 + ch) % 2]
                                    P.wait("te", sVE, wdone)
                                    P.wait("te", sG2, xld)
                                    P.wait("te", sSE, qb_se[(h * 2 + ch) % 2] if True else 0)
                                    nmm = 1 if "mm1" in dbg else HC
                                    def mmf(e, h=h, ch=ch, b=b, bank=bank, nmm=nmm):
                                        r = None
                                        for hc in range(nmm):
                                            r = e.matmul(
                                                out=bank[:],
                                                lhsT=W_sb[hc][:, h * 128:(h + 1) * 128],
                                                rhs=xqb[b][hc][:, ch * 512:(ch + 1) * 512],
                                                start=(hc == 0), stop=(hc == nmm - 1))
                                        return r
                                    mmdone = P.op("te", mmf, inc=sPE)
                                    gcol = qt * 1024 + ch * 512
                                    dstd = q_s[h] if kind == "q" else k_s[h]
                                    rope_evac(bank, h * 2 + ch, gcol, kind == "q",
                                              dstd, slice(gcol, gcol + 512), mmdone)
                        else:
                            for tt in range(8):
                                for dsl in range(4):
                                    i4 = (tt * 4 + dsl) % 2
                                    bank = pb[i4]
                                    P.wait("te", sVE, wdone)
                                    P.wait("te", sG2, xld)
                                    P.wait("te", sSE)
                                    def mmf(e, tt=tt, dsl=dsl, b=b, bank=bank):
                                        r = None
                                        for hc in range(HC):
                                            r = e.matmul(
                                                out=bank[:],
                                                lhsT=xqb[b][hc][:, tt * 128:(tt + 1) * 128],
                                                rhs=W_sb[hc][:, dsl * 512:(dsl + 1) * 512],
                                                start=(hc == 0), stop=(hc == HC - 1))
                                        return r
                                    mmdone = P.op("te", mmf, inc=sPE)
                                    P.wait("se", sPE, mmdone)
                                    slot = vb_i[0]
                                    P.wait("se", sSP, vb_dma[slot])
                                    sdone = P.op("se", lambda e, bank=bank, slot=slot:
                                                 e.activation(out=vb[slot][:],
                                                              in_=bank[:],
                                                              func=AF.Copy), inc=sSE)
                                    P.wait("ge", sSE, sdone)
                                    ktg = qt * 8 + tt
                                    for sub in range(4):
                                        h = dsl * 4 + sub
                                        P.dma("ge", v_s[h][:, ktg * 128:(ktg + 1) * 128],
                                              vb[slot][:, sub * 128:(sub + 1) * 128], sSP)
                                    vb_dma[slot] = sSP.n
                                    vb_i[0] = (slot + 1) % 4
                P.wait("ge", sSP)
                P.barrier()
        rope_stack.close()

        if want("wq"):
            with contextlib.ExitStack() as ps:
                Tp = lambda nm, sh, dt: ps.enter_context(nc.sbuf_tensor(nm, sh, dt))
                wst2 = [Tp(f"wq_st{i}", [128, II], F32) for i in range(2)]
                macc = Tp("wq_macc", [128, II], F32)
                wb_ = Tp("wq_wb", [128, II], BF16)
                rws_bc = Tp("wq_rws", [128, II], BF16)
                cols2 = Tp("wq_cols2", [128, 2], F32)
                rws_row = Tp("wq_rwsr", [1, II], F32)
                ws_row = Tp("wq_wsr", [1, II], F32)
                rws_row_bf = Tp("wq_rwsrb", [1, II], BF16)
                qtmp = qfp8_tmps(Tp, "wq", 1408)
                qout = Tp("wq_qout", [128, II], BF16)

                for wsrc, ntile, ncols, qdst, wsdst in (
                        (wgT, HC, II, qwg_s, wsg_s),
                        (wuT, HC, II, qwu_s, wsu_s),
                        (wdT, IC, H, qwd_s, wsd_s)):
                    vm = []
                    for it in range(ntile):
                        b = it % 2
                        if it >= 2:
                            P.wait("sy", sVE, vm[it - 2])
                        P.dma("sy", wst2[b][:, 0:ncols],
                              wsrc[it * 128:(it + 1) * 128, :], sLD)
                        P.wait("ve", sLD)
                        if it == 0:
                            vm.append(P.op("ve", lambda e, b=b, ncols=ncols:
                                 e.scalar_tensor_tensor(
                                     out=macc[:, 0:ncols], in0=wst2[b][:, 0:ncols],
                                     scalar=-1.0, in1=wst2[b][:, 0:ncols],
                                     op0=ALU.mult, op1=ALU.max), inc=sVE))
                        else:
                            P.op("ve", lambda e, b=b, ncols=ncols: e.tensor_tensor(
                                out=macc[:, 0:ncols], in0=macc[:, 0:ncols],
                                in1=wst2[b][:, 0:ncols], op=ALU.max))
                            vm.append(P.op("ve", lambda e, b=b, ncols=ncols:
                                 e.scalar_tensor_tensor(
                                     out=macc[:, 0:ncols], in0=wst2[b][:, 0:ncols],
                                     scalar=-1.0, in1=macc[:, 0:ncols],
                                     op0=ALU.mult, op1=ALU.max), inc=sVE))
                    if "wqs1" in dbg:
                        continue
                    for sl in range(ncols // 128):
                        P.wait("te", sVE)
                        trm = P.op("te", lambda e, sl=sl: e.matmul(
                            out=pb[5][:, 0:128],
                            lhsT=macc[:, sl * 128:(sl + 1) * 128], rhs=ident_f[:],
                            start=True, stop=True), inc=sPE)
                        P.wait("ve", sPE, trm)
                        P.vsop(lambda e: e.tensor_reduce(
                            out=cols2[:, 0:1], in_=pb[5][:, 0:128], axis=AX.X,
                            op=ALU.max))
                        P.vsop(lambda e: e.tensor_scalar(
                            out=cols2[:, 1:2], in0=cols2[:, 0:1], scalar1=1.0 / 448,
                            scalar2=None, op0=ALU.mult))
                        P.vsop(lambda e: e.reciprocal(
                            out=cols2[:, 0:1], in_=cols2[:, 0:1]))
                        vc = P.vsop(lambda e: e.tensor_scalar(
                            out=cols2[:, 0:1], in0=cols2[:, 0:1], scalar1=448.0,
                            scalar2=None, op0=ALU.mult))
                        P.wait("te", sVE, vc)
                        P.op("te", lambda e: e.matmul(
                            out=pb[6][0:1, 0:128], lhsT=cols2[:, 0:1],
                            rhs=ident_f[:], start=True, stop=True), inc=sPE)
                        trr = P.op("te", lambda e: e.matmul(
                            out=pb[7][0:1, 0:128], lhsT=cols2[:, 1:2],
                            rhs=ident_f[:], start=True, stop=True), inc=sPE)
                        P.wait("ve", sPE, trr)
                        P.vsop(lambda e, sl=sl: e.tensor_copy(
                            out=rws_row[:, sl * 128:(sl + 1) * 128],
                            in_=pb[6][0:1, 0:128]))
                        P.vsop(lambda e, sl=sl: e.tensor_copy(
                            out=ws_row[:, sl * 128:(sl + 1) * 128],
                            in_=pb[7][0:1, 0:128]))
                    vrow = P.vsop(lambda e, ncols=ncols: e.tensor_copy(
                        out=rws_row_bf[:, 0:ncols], in_=rws_row[:, 0:ncols]))
                    P.wait("ge", sVE, vrow)
                    P.wait("ge", sG2)
                    P.dma("ge", rws_d[0:1, 0:ncols], rws_row_bf[0:1, 0:ncols], sSP)
                    P.dma("ge", wsdst[0:1, 0:ncols], ws_row[0:1, 0:ncols], sSP)
                    P.wait("ge", sSP)
                    P.dma("ge", rws_bc[:, 0:ncols],
                          rws_d[0:1, 0:ncols].partition_broadcast(128).squeeze(1),
                          sG2)
                    qdma = [0] * 2
                    for it in range(ntile):
                        b = it % 2
                        P.wait("sy", sVE)
                        P.dma("sy", wst2[b][:, 0:ncols],
                              wsrc[it * 128:(it + 1) * 128, :], sLD)
                        P.wait("se", sLD)
                        scast = P.op("se", lambda e, b=b, ncols=ncols: e.activation(
                            out=wb_[:, 0:ncols], in_=wst2[b][:, 0:ncols],
                            func=AF.Copy), inc=sSE)
                        P.wait("ve", sSE, scast)
                        P.wait("ve", sSP, qdma[b])
                        P.wait("ve", sG2)
                        P.op("ve", lambda e, ncols=ncols: e.tensor_tensor(
                            out=wb_[:, 0:ncols], in0=wb_[:, 0:ncols],
                            in1=rws_bc[:, 0:ncols], op=ALU.mult))
                        nq = ncols // 1408 if ncols % 1408 == 0 else None
                        if nq is None:
                            nq, w = 2, ncols // 2
                        else:
                            w = 1408
                        for hf in range(nq):
                            emit_qfp8(P, wb_[:, hf * w:(hf + 1) * w],
                                      qout[:, hf * w:(hf + 1) * w],
                                      qtmp, w, inc=sVE)
                        vq = sVE.n
                        P.wait("ge", sVE, vq)
                        P.dma("ge", qdst[it], qout[:, 0:ncols], sSP)
                        qdma[b] = sSP.n
                P.wait("ge", sSP)
                P.barrier()

        ctx_stack = contextlib.ExitStack()
        Tc = lambda nm, sh, dt: ctx_stack.enter_context(nc.sbuf_tensor(nm, sh, dt))
        ctx_fm = [Tc(f"ctx{i}", [128, NTOK], BF16) for i in range(NH)]

        if want("attn"):
            with contextlib.ExitStack() as ps:
                Tp = lambda nm, sh, dt: ps.enter_context(nc.sbuf_tensor(nm, sh, dt))
                K_sb = [Tp(f"Ksb{i}", [128, S], BF16) for i in range(2)]
                V_sb = [Tp(f"Vsb{i}", [128, S], BF16) for i in range(2)]
                Q_sb = [Tp(f"Qsb{i}", [128, NTOK], BF16) for i in range(2)]
                PT = Tp("PT", [128, KT * 512], BF16)
                ACC = Tp("ACC", [128, 512], F32)
                ACCb = Tp("ACCb", [128, 512], BF16)
                lrow = Tp("lrow", [1, 512], F32)
                lrow_bf = Tp("lrow_bf", [1, 512], BF16)
                rlb = Tp("rlb", [128, 512], BF16)
                msk = Tp("msk", [128, 32 * 512], BF16)
                bstage = Tp("bstage", [128, 512], F32)

                for c in range(4):
                    P.wait("sy", sVE)
                    P.dma("sy", bstage[:], bmask[c], sLD)
                    P.wait("ve", sLD)
                    for j in range(8):
                        kt = 8 * c + j
                        P.op("ve", lambda e, c=c, j=j, kt=kt: e.tensor_scalar(
                            out=msk[:, (c * 8 + j) * 512:(c * 8 + j + 1) * 512],
                            in0=bstage[:], scalar1=float(128 * kt), scalar2=None,
                            op0=ALU.is_ge), inc=sVE)

                head_pe_end = [0, 0]
                ctx_pe_mark = [0] * KT
                exp_mark = [0] * KT
                mask_mark = [0] * KT

                for h in range(NH):
                    hb = h % 2
                    P.wait("sy", sPE, head_pe_end[hb])
                    P.dma("sy", K_sb[hb][:], k_s[h], sLD)
                    P.dma("sy", V_sb[hb][:], v_s[h], sLD)
                    P.dma("sy", Q_sb[hb][:], q_s[h], sLD)
                    ld_mark = sLD.n
                    for c in range(4):
                        ktmax = 8 * c + 8
                        scale_done_prev = sVE.n
                        for kt in range(ktmax):
                            bank = pb[kt % 2]
                            P.wait("te", sLD, ld_mark)
                            if kt >= 2:
                                P.wait("te", sSE, exp_mark[kt - 2])
                            if kt == 0:
                                P.wait("te", sVE, scale_done_prev)
                            def mm_s(e, hb=hb, c=c, kt=kt, bank=bank):
                                return e.matmul(
                                    out=bank[:],
                                    lhsT=K_sb[hb][:, kt * 128:(kt + 1) * 128],
                                    rhs=Q_sb[hb][:, c * 512:(c + 1) * 512],
                                    start=True, stop=True)
                            mmdone = P.op("te", mm_s, inc=sPE)
                            P.wait("se", sPE, max(mmdone, ctx_pe_mark[kt]))
                            pt_sl = (slice(0, 128), slice(kt * 512, (kt + 1) * 512))
                            exp_mark[kt] = P.op(
                                "se", lambda e, bank=bank, pt_sl=pt_sl: e.activation(
                                    out=PT[pt_sl], in_=bank[:], func=AF.Exp,
                                    scale=csc[:]), inc=sSE)
                            P.wait("ve", sSE, exp_mark[kt])
                            if kt >= 8 * c:
                                j = kt - 8 * c
                                mask_mark[kt] = P.op(
                                    "ve", lambda e, pt_sl=pt_sl, c=c, j=j:
                                    e.tensor_tensor(
                                        out=PT[pt_sl], in0=PT[pt_sl],
                                        in1=msk[:, (c * 8 + j) * 512:(c * 8 + j + 1) * 512],
                                        op=ALU.mult), inc=sVE)
                            if kt == 0:
                                P.op("ve", lambda e, pt_sl=pt_sl: e.tensor_copy(
                                    out=ACC[:], in_=PT[pt_sl]), inc=sVE)
                            else:
                                P.op("ve", lambda e, pt_sl=pt_sl: e.tensor_tensor(
                                    out=ACC[:], in0=ACC[:], in1=PT[pt_sl],
                                    op=ALU.add), inc=sVE)
                            if kt >= 8 * c:
                                P.wait("te", sVE, mask_mark[kt])
                            else:
                                P.wait("te", sSE, exp_mark[kt])
                            def mm_c(e, hb=hb, kt=kt, pt_sl=pt_sl, ktmax=ktmax):
                                return e.matmul(
                                    out=pb[2][:],
                                    lhsT=V_sb[hb][:, kt * 128:(kt + 1) * 128],
                                    rhs=PT[pt_sl], start=(kt == 0),
                                    stop=(kt == ktmax - 1))
                            ctx_pe_mark[kt] = P.op("te", mm_c, inc=sPE)
                        vacc = P.op("ve", lambda e: e.tensor_copy(
                            out=ACCb[:], in_=ACC[:]), inc=sVE)
                        P.wait("te", sVE, vacc)
                        lmm = P.op("te", lambda e: e.matmul(
                            out=pb[3][0:1, :], lhsT=ones_bf[:, 0:1], rhs=ACCb[:],
                            start=True, stop=True), inc=sPE)
                        P.wait("ve", sPE, lmm)
                        P.vsop(lambda e: e.tensor_copy(out=lrow[:], in_=pb[3][0:1, :]))
                        P.vsop(lambda e: e.reciprocal(out=lrow[:], in_=lrow[:]))
                        vrl = P.vsop(lambda e: e.tensor_copy(
                            out=lrow_bf[:], in_=lrow[:]))
                        P.wait("te", sVE, vrl)
                        bmm = P.op("te", lambda e: e.matmul(
                            out=pb[4][:], lhsT=ones_bf[0:1, :], rhs=lrow_bf[:],
                            start=True, stop=True), inc=sPE)
                        P.wait("ve", sPE, bmm)
                        P.op("ve", lambda e: e.tensor_copy(out=rlb[:], in_=pb[4][:]))
                        P.op("ve", lambda e, h=h, c=c: e.tensor_tensor(
                            out=ctx_fm[h][:, c * 512:(c + 1) * 512], in0=pb[2][:],
                            in1=rlb[:], op=ALU.mult), inc=sVE)
                    head_pe_end[hb] = sPE.n
                if ctx_s is not None:
                    P.wait("ge", sVE)
                    for h in range(NH):
                        P.dma("ge", ctx_s[h], ctx_fm[h][:], sSP)
                    P.wait("ge", sSP)
                P.barrier()

        if want("oproj"):
            with contextlib.ExitStack() as ps:
                Tp = lambda nm, sh, dt: ps.enter_context(nc.sbuf_tensor(nm, sh, dt))
                Wo_sb = [Tp(f"Wo{i}", [128, H], BF16) for i in range(HC)]
                wst3 = [Tp(f"wo_st{i}", [128, H], F32) for i in range(2)]
                xres = [Tp(f"xres{i}", [128, H], F32) for i in range(2)]
                x1pre = Tp("x1pre", [128, H], F32)
                lnb = dict(sq=Tp("ln_sq", [128, H], F32),
                           m=Tp("ln_m", [128, 1], F32), v=Tp("ln_v", [128, 1], F32),
                           m2=Tp("ln_m2", [128, 1], F32),
                           rstd=Tp("ln_rs", [128, 1], F32))
                x1t = Tp("x1t", [128, H], BF16)
                rxc = Tp("rxc", [128, 1], F32)
                rscl = Tp("rscl", [128, 1], F32)
                qxr = Tp("qxr", [128, H], BF16)
                qxt = Tp("qxt", [128, H], BF16)
                qtmp3 = qfp8_tmps(Tp, "p3", 1024)
                qxT_t = [Tp(f"qxT{i}", [128, 128], BF16) for i in range(4)]

                wocasts = []
                for hc in range(HC):
                    b = hc % 2
                    if hc >= 2:
                        P.wait("sy", sVE, wocasts[hc - 2])
                    P.dma("sy", wst3[b][:], woT[hc * 128:(hc + 1) * 128, :], sLD)
                    P.wait("ve", sLD)
                    wocasts.append(P.op("ve", lambda e, hc=hc, b=b: e.tensor_copy(
                        out=Wo_sb[hc][:], in_=wst3[b][:]), inc=sVE))
                wodone = sVE.n

                qxT_dma = [0] * 4
                x1_dma = [0]
                for tt in range(NTT):
                    b = tt % 2
                    P.wait("sy", sVE)
                    P.dma("sy", xres[b][:], xrows[tt * 128:(tt + 1) * 128, :], sLD)
                    xldm = sLD.n
                    for osl in range(4):
                        bank = pb[osl % 2]
                        P.wait("te", sVE, max(wodone, sVE.n))
                        def mmo(e, tt=tt, osl=osl, bank=bank):
                            r = None
                            for dc in range(HC):
                                r = e.matmul(out=bank[:],
                                             lhsT=ctx_fm[dc][:, tt * 128:(tt + 1) * 128],
                                             rhs=Wo_sb[dc][:, osl * 512:(osl + 1) * 512],
                                             start=(dc == 0), stop=(dc == HC - 1))
                            return r
                        mmd = P.op("te", mmo, inc=sPE)
                        P.wait("ve", sPE, mmd)
                        P.wait("ve", sLD, xldm)
                        P.op("ve", lambda e, b=b, osl=osl, bank=bank: e.tensor_tensor(
                            out=x1pre[:, osl * 512:(osl + 1) * 512], in0=bank[:],
                            in1=xres[b][:, osl * 512:(osl + 1) * 512], op=ALU.add),
                            inc=sVE)
                    P.wait("ve", sSP, x1_dma[0])
                    lndone = emit_ln(P, x1pre[:], x1t[:], lnb)
                    P.wait("ge", sVE, lndone)
                    P.dma("ge", x1_s[tt], x1t[:], sSP)
                    x1_dma[0] = sSP.n
                    P.vsop(lambda e: e.tensor_reduce(
                        out=rxc[:], in_=x1t[:], axis=AX.X, op=ALU.max,
                        apply_absolute_value=True))
                    P.vsop(lambda e, tt=tt: e.tensor_scalar(
                        out=ax_col[:, tt:tt + 1], in0=rxc[:], scalar1=1.0 / 448,
                        scalar2=None, op0=ALU.mult))
                    P.vsop(lambda e: e.reciprocal(out=rscl[:], in_=rxc[:]))
                    P.vsop(lambda e: e.tensor_scalar(
                        out=rscl[:], in0=rscl[:], scalar1=448.0, scalar2=None,
                        op0=ALU.mult))
                    P.wait("ve", sPE)
                    P.op("ve", lambda e: e.tensor_scalar(
                        out=qxr[:], in0=x1t[:], scalar1=rscl[:], scalar2=None,
                        op0=ALU.mult))
                    for hf in range(2):
                        emit_qfp8(P, qxr[:, hf * 1024:(hf + 1) * 1024],
                                  qxt[:, hf * 1024:(hf + 1) * 1024], qtmp3, 1024,
                                  inc=sVE)
                    qdone = sVE.n
                    for hc in range(HC):
                        slot = hc % 4
                        P.wait("te", sVE, qdone)
                        if hc >= 2:
                            P.wait("te", sVE)
                        trm = P.op("te", lambda e, hc=hc: e.matmul(
                            out=pb[4 + hc % 2][:, 0:128],
                            lhsT=qxt[:, hc * 128:(hc + 1) * 128], rhs=ident_bf[:],
                            start=True, stop=True), inc=sPE)
                        P.wait("ve", sPE, trm)
                        P.wait("ve", sSP, qxT_dma[slot])
                        vev = P.op("ve", lambda e, hc=hc, slot=slot: e.tensor_copy(
                            out=qxT_t[slot][:], in_=pb[4 + hc % 2][:, 0:128]),
                            inc=sVE)
                        P.wait("ge", sVE, vev)
                        P.dma("ge", qx_s[hc][:, tt * 128:(tt + 1) * 128],
                              qxT_t[slot][:], sSP)
                        qxT_dma[slot] = sSP.n
                P.wait("ge", sSP)
                P.barrier()
        ctx_stack.close()

        if want("ffna"):
            with contextlib.ExitStack() as ps:
                Tp = lambda nm, sh, dt: ps.enter_context(nc.sbuf_tensor(nm, sh, dt))
                qx_fm = [Tp(f"qxf{i}", [128, NTOK], BF16) for i in range(HC)]
                qwg_sl = [[Tp(f"qwg{b}_{i}", [128, 512], BF16) for i in range(HC)]
                          for b in range(2)]
                qwu_sl = [[Tp(f"qwu{b}_{i}", [128, 512], BF16) for i in range(HC)]
                          for b in range(2)]
                wsg_t = [Tp(f"wsgt{b}", [128, 512], F32) for b in range(2)]
                wsu_t = [Tp(f"wsut{b}", [128, 512], F32) for b in range(2)]
                m1 = Tp("m1", [128, 512], F32)
                s1 = Tp("s1", [128, 512], F32)
                s1w = Tp("s1w", [128, 512], F32)
                tbf = [Tp(f"tbf{i}", [128, 512], BF16) for i in range(4)]
                tcol = Tp("tcol", [128, 1], F32)

                for hc in range(HC):
                    P.dma("sy", qx_fm[hc][:], qx_s[hc], sLD)
                qxld = sLD.n
                P.wait("te", sLD, qxld)

                tb_dma = [0] * 4
                tb_i = [0]
                for isl in range(ISL):
                    b = isl % 2
                    pe_mark = sPE.n
                    for hc in range(HC):
                        if isl >= 2:
                            P.wait("sy", sPE, pe_mark)
                        P.dma("sy", qwg_sl[b][hc][:],
                              qwg_s[hc][:, isl * 512:(isl + 1) * 512], sG2)
                        P.dma("sy", qwu_sl[b][hc][:],
                              qwu_s[hc][:, isl * 512:(isl + 1) * 512], sG2)
                    P.dma("sy", wsg_t[b][:],
                          wsg_s[0:1, isl * 512:(isl + 1) * 512]
                          .partition_broadcast(128).squeeze(1), sG2)
                    P.dma("sy", wsu_t[b][:],
                          wsu_s[0:1, isl * 512:(isl + 1) * 512]
                          .partition_broadcast(128).squeeze(1), sG2)
                    wld = sG2.n
                    for tt in range(NTT):
                        bg = pb[tt % 2]
                        bu = pb[2 + tt % 2]
                        P.wait("te", sG2, wld)
                        P.wait("te", sVE)
                        def mmg(e, b=b, tt=tt, bg=bg):
                            r = None
                            for hc in range(HC):
                                r = e.matmul(out=bg[:],
                                             lhsT=qx_fm[hc][:, tt * 128:(tt + 1) * 128],
                                             rhs=qwg_sl[b][hc][:],
                                             start=(hc == 0), stop=(hc == HC - 1))
                            return r
                        mg = P.op("te", mmg, inc=sPE)
                        def mmu(e, b=b, tt=tt, bu=bu):
                            r = None
                            for hc in range(HC):
                                r = e.matmul(out=bu[:],
                                             lhsT=qx_fm[hc][:, tt * 128:(tt + 1) * 128],
                                             rhs=qwu_sl[b][hc][:],
                                             start=(hc == 0), stop=(hc == HC - 1))
                            return r
                        mu = P.op("te", mmu, inc=sPE)
                        P.wait("ve", sPE, mg)
                        vm1 = P.op("ve", lambda e, b=b, tt=tt, bg=bg:
                                   e.scalar_tensor_tensor(
                                       out=m1[:], in0=bg[:],
                                       scalar=ax_col[:, tt:tt + 1], in1=wsg_t[b][:],
                                       op0=ALU.mult, op1=ALU.mult), inc=sVE)
                        P.wait("se", sVE, vm1)
                        ss1 = P.op("se", lambda e: e.activation(
                            out=s1[:], in_=m1[:], func=AF.Silu), inc=sSE)
                        P.wait("ve", sSE, ss1)
                        P.wait("ve", sPE, mu)
                        slot = tb_i[0]
                        P.wait("ve", sSP, tb_dma[slot])
                        P.op("ve", lambda e, b=b: e.tensor_tensor(
                            out=s1w[:], in0=s1[:], in1=wsu_t[b][:], op=ALU.mult))
                        P.op("ve", lambda e, tt=tt, bu=bu, slot=slot:
                             e.scalar_tensor_tensor(
                                 out=tbf[slot][:], in0=bu[:],
                                 scalar=ax_col[:, tt:tt + 1], in1=s1w[:],
                                 op0=ALU.mult, op1=ALU.mult))
                        P.vsop(lambda e, slot=slot: e.tensor_reduce(
                            out=tcol[:], in_=tbf[slot][:], axis=AX.X, op=ALU.max,
                            apply_absolute_value=True))
                        if isl == 0:
                            vt = P.vsop(lambda e, tt=tt: e.tensor_copy(
                                out=rmx_t[:, tt:tt + 1], in_=tcol[:]))
                        else:
                            vt = P.vsop(lambda e, tt=tt: e.tensor_tensor(
                                out=rmx_t[:, tt:tt + 1], in0=rmx_t[:, tt:tt + 1],
                                in1=tcol[:], op=ALU.max))
                        P.wait("ge", sVE, vt)
                        P.dma("ge", t_s[tt][:, isl * 512:(isl + 1) * 512],
                              tbf[slot][:], sSP)
                        tb_dma[slot] = sSP.n
                        tb_i[0] = (slot + 1) % 4
                P.vsop(lambda e: e.tensor_scalar(
                    out=at_col[:], in0=rmx_t[:], scalar1=1.0 / 448, scalar2=None,
                    op0=ALU.mult))
                P.wait("ge", sSP)
                P.barrier()

        if want("ffnb"):
            with contextlib.ExitStack() as ps:
                Tp = lambda nm, sh, dt: ps.enter_context(nc.sbuf_tensor(nm, sh, dt))
                wsd_bc = Tp("wsd_bc", [128, H], F32)
                tin = [Tp(f"tin{i}", [128, II], BF16) for i in range(2)]
                rsc = Tp("rsc", [128, 1], F32)
                qtr = Tp("qtr", [128, II], BF16)
                qtmp5 = qfp8_tmps(Tp, "p5", 2816)
                qt_q = Tp("qt_q", [128, II], BF16)
                qt_fm = [[Tp(f"qtf{t}_{i}", [128, 128], BF16) for i in range(IC)]
                         for t in range(2)]
                qwd_sl = [Tp(f"qwd{i}", [128, 512], BF16) for i in range(IC)]
                x1in = [Tp(f"x1in{i}", [128, H], BF16) for i in range(2)]
                y2pre = [Tp(f"y2p{i}", [128, H], F32) for i in range(2)]
                lnb5 = dict(sq=Tp("l5_sq", [128, H], F32),
                            m=Tp("l5_m", [128, 1], F32), v=Tp("l5_v", [128, 1], F32),
                            m2=Tp("l5_m2", [128, 1], F32),
                            rstd=Tp("l5_rs", [128, 1], F32))
                yout = [Tp(f"yo{i}", [128, H], F32) for i in range(2)]

                P.dma("sy", wsd_bc[:],
                      wsd_s[0:1, :].partition_broadcast(128).squeeze(1), sLD)
                wsld = sLD.n

                y_dma = [0, 0]
                for sup in range(8):
                    qt_done = []
                    sup_pe_mark = sPE.n
                    for ttl in range(2):
                        tt = sup * 2 + ttl
                        b = tt % 2
                        P.wait("sy", sVE)
                        P.dma("sy", tin[b][:], t_s[tt], sLD)
                        P.dma("sy", x1in[ttl][:], x1_s[tt], sLD)
                        tld = sLD.n
                        P.wait("ve", sLD, tld)
                        P.wait("ve", sPE, sup_pe_mark)
                        P.vsop(lambda e, tt=tt: e.reciprocal(
                            out=rsc[:], in_=rmx_t[:, tt:tt + 1]))
                        P.vsop(lambda e: e.tensor_scalar(
                            out=rsc[:], in0=rsc[:], scalar1=448.0, scalar2=None,
                            op0=ALU.mult))
                        P.op("ve", lambda e, b=b: e.tensor_scalar(
                            out=qtr[:], in0=tin[b][:], scalar1=rsc[:], scalar2=None,
                            op0=ALU.mult))
                        for hf in range(2):
                            emit_qfp8(P, qtr[:, hf * 2816:(hf + 1) * 2816],
                                      qt_q[:, hf * 2816:(hf + 1) * 2816], qtmp5, 2816,
                                      inc=sVE)
                        vq = sVE.n
                        for ic in range(IC):
                            P.wait("te", sVE, vq)
                            if ic >= 2:
                                P.wait("te", sVE)
                            trm = P.op("te", lambda e, ic=ic: e.matmul(
                                out=pb[4 + ic % 2][:, 0:128],
                                lhsT=qt_q[:, ic * 128:(ic + 1) * 128],
                                rhs=ident_bf[:], start=True, stop=True), inc=sPE)
                            P.wait("ve", sPE, trm)
                            P.op("ve", lambda e, ttl=ttl, ic=ic: e.tensor_copy(
                                out=qt_fm[ttl][ic][:], in_=pb[4 + ic % 2][:, 0:128]),
                                inc=sVE)
                        qt_done.append(sVE.n)
                    for hsl in range(4):
                        pe_mark = sPE.n
                        ld_marks = []
                        for ic in range(IC):
                            if sup > 0 or hsl > 0:
                                P.wait("sy", sPE, pe_mark)
                            P.dma("sy", qwd_sl[ic][:],
                                  qwd_s[ic][:, hsl * 512:(hsl + 1) * 512], sG2)
                            ld_marks.append(sG2.n)
                        for ttl in range(2):
                            tt = sup * 2 + ttl
                            bank = pb[ttl % 2]
                            P.wait("te", sVE, qt_done[ttl])
                            P.wait("te", sG2, ld_marks[-1])
                            P.wait("te", sVE)
                            def mmd(e, ttl=ttl, bank=bank):
                                r = None
                                for ic in range(IC):
                                    r = e.matmul(out=bank[:], lhsT=qt_fm[ttl][ic][:],
                                                 rhs=qwd_sl[ic][:],
                                                 start=(ic == 0), stop=(ic == IC - 1))
                                return r
                            md = P.op("te", mmd, inc=sPE)
                            P.wait("ve", sPE, md)
                            P.wait("ve", sLD, wsld)
                            cols = slice(hsl * 512, (hsl + 1) * 512)
                            P.op("ve", lambda e, tt=tt, bank=bank, cols=cols:
                                 e.scalar_tensor_tensor(
                                     out=lnb5["sq"][:, cols], in0=bank[:],
                                     scalar=at_col[:, tt:tt + 1],
                                     in1=wsd_bc[:, cols],
                                     op0=ALU.mult, op1=ALU.mult))
                            P.op("ve", lambda e, ttl=ttl, cols=cols:
                                 e.tensor_tensor(
                                     out=y2pre[ttl][:, cols],
                                     in0=lnb5["sq"][:, cols],
                                     in1=x1in[ttl][:, cols], op=ALU.add),
                                 inc=sVE)
                    for ttl in range(2):
                        tt = sup * 2 + ttl
                        b = tt % 2
                        P.wait("ve", sSP, y_dma[b])
                        lnd = emit_ln(P, y2pre[ttl][:], yout[b][:], lnb5)
                        P.wait("ge", sVE, lnd)
                        P.dma("ge", y_out[tt * 128:(tt + 1) * 128, :], yout[b][:],
                              sSP)
                        y_dma[b] = sSP.n
                P.barrier()

        P.wait("sy", sSP)
        P.wait("ge", sSP)

        block = es.enter_context(nc.Block())
        P.emit(block)
    return nc


def own_positions(p):
    return np.concatenate([np.arange(1024 * i + 512 * p, 1024 * i + 512 * p + 512)
                           for i in range(4)])


def make_inputs(x_full, Wq, Wk, Wv, Wo, wg, wu, wd, dn_idx):
    f32 = np.float32
    dn_inv = np.argsort(np.asarray(dn_idx))
    shared = dict(
        wqT=np.ascontiguousarray(Wq.T, dtype=f32),
        wkT=np.ascontiguousarray(Wk.T, dtype=f32),
        wvT=np.ascontiguousarray(Wv.T, dtype=f32),
        woT=np.ascontiguousarray(Wo.T, dtype=f32),
        wgT=np.ascontiguousarray(wg.T, dtype=f32),
        wuT=np.ascontiguousarray(wu.T, dtype=f32),
        wdT=np.ascontiguousarray(wd.T[dn_inv], dtype=f32),
        ident=np.eye(128, dtype=f32),
        ones=np.ones((128, 128), dtype=f32),
        iota_t=np.broadcast_to(np.arange(S, dtype=f32), (64, S)).copy(),
        iota_j=np.arange(64, dtype=f32).reshape(64, 1),
    )
    in_maps = []
    for c in range(8):
        b, p = c // 2, c % 2
        pos = own_positions(p)
        xb = np.asarray(x_full[b], dtype=f32)
        m = dict(shared)
        m["xT"] = np.ascontiguousarray(xb.T)
        m["xqT"] = np.ascontiguousarray(xb[pos].T)
        m["xrows"] = np.ascontiguousarray(xb[pos])
        m["iota_q"] = np.broadcast_to(pos.astype(f32), (64, NTOK)).copy()
        bm = np.empty((4, 128, 512), dtype=f32)
        kl = np.arange(128, dtype=f32).reshape(128, 1)
        for ch in range(4):
            bm[ch] = pos[ch * 512:(ch + 1) * 512].astype(f32).reshape(1, 512) - kl
        m["bmask"] = bm
        in_maps.append(m)
    return in_maps


def kernel(L0_attn_hidden_states, L0_attn_Wq, L0_attn_Wk, L0_attn_Wv, L0_attn_Wo,
           L0_ffn_w_gate, L0_ffn_w_up, L0_ffn_w_down,
           L0_ffn_up_reorder_index, L0_ffn_down_reorder_index):
    x = np.asarray(L0_attn_hidden_states, dtype=np.float32)
    in_maps = make_inputs(x, np.asarray(L0_attn_Wq), np.asarray(L0_attn_Wk),
                          np.asarray(L0_attn_Wv), np.asarray(L0_attn_Wo),
                          np.asarray(L0_ffn_w_gate), np.asarray(L0_ffn_w_up),
                          np.asarray(L0_ffn_w_down),
                          np.asarray(L0_ffn_down_reorder_index))
    nc = build_program()
    res = run_bass_kernel_spmd(nc, in_maps, core_ids=list(range(8)))
    out = np.empty((B, S, H), dtype=np.float32)
    for c in range(8):
        b, p = c // 2, c % 2
        out[b][own_positions(p)] = res.results[c]["y"]
    return out
